# version 01
# baseline (speedup 1.0000x reference)
import math
import contextlib
import numpy as np
import ml_dtypes
import concourse.bass as bass
import concourse.mybir as mybir
from concourse.bass_utils import run_bass_kernel_spmd
from concourse.bass_core_rust import IndirectOffsetOnAxis

F32 = mybir.dt.float32
BF16 = mybir.dt.bfloat16
I32 = mybir.dt.int32
AF = mybir.ActivationFunctionType
ALU = mybir.AluOpType
AX = mybir.AxisListType
bfnp = ml_dtypes.bfloat16

D = 1024
S = 4096
NJ = 8
TILE = 512
NTILE = S // TILE
NCH = TILE // 128
MAGIC = 1.5 * 2.0 ** 23
TWO_PI = 2.0 * math.pi


class Seq:

    def __init__(self, nc, sem):
        self.nc, self.sem, self.val = nc, sem, 0
        self.q = {e: [] for e in ("sync", "gpsimd", "scalar", "vector", "tensor")}

    def do(self, eng, fn, inc=1):
        self.q[eng].append((self.val, fn, inc))
        self.val += inc

    def dma(self, eng, fn):
        self.do(eng, fn, inc=16)

    def emit(self, block):
        final = self.val

        def mk(engname):
            def body(eng):
                for wait, fn, inc in self.q[engname]:
                    if wait > 0:
                        eng.wait_ge(self.sem, wait)
                    fn(eng).then_inc(self.sem, inc)
                if engname == "sync":
                    eng.wait_ge(self.sem, final)
            return body

        for name in self.q:
            if self.q[name] or name == "sync":
                getattr(block, name)(mk(name))


def build_program():
    nc = bass.Bass("TRN2", target_bir_lowering=False)

    x_h = nc.dram_tensor("x", [S, D], F32, kind="ExternalInput").ap()
    out_h = nc.dram_tensor("out", [S, D], F32, kind="ExternalOutput").ap()
    wT = {}
    for nm in ("wk", "wv", "wq", "wkp", "wqp", "wo"):
        wT[nm] = nc.dram_tensor(nm + "T", [D, D], F32, kind="ExternalInput").ap()
    vec = {}
    for nm in ("bk", "bv", "bq", "bkp", "bqp", "ps", "lng", "lnb", "bo"):
        vec[nm] = nc.dram_tensor(nm, [D], F32, kind="ExternalInput").ap()
    hyp_h = nc.dram_tensor("hyp", [D, 6], F32, kind="ExternalInput").ap()
    ew_h = nc.dram_tensor("ew", [64, D, D], F32, kind="ExternalInput").ap()
    ew_rows = ew_h.rearrange("e d k -> (e d k)").rearrange("(r c) -> r c", c=NJ * D)

    ident_bf_h = nc.inline_tensor(np.eye(128, dtype=bfnp), "identbf")
    ident8_h = nc.inline_tensor(np.eye(8, dtype=np.float32), "ident8f")
    iota_col_h = nc.inline_tensor(np.arange(128, dtype=np.float32).reshape(128, 1), "iotacol")
    pw8_h = nc.inline_tensor(np.tile((2.0 ** np.arange(6)).astype(np.float32), (8, 1)), "pw8")
    iota6_h = nc.inline_tensor(np.tile(np.arange(6, dtype=np.float32), (8, 1)), "iota6")
    iota6b_h = nc.inline_tensor(
        np.tile(np.arange(6, dtype=np.float32) - 1.0e6, (8, 1)), "iota6b")
    oh_np = np.zeros((128, 4 * NCH), dtype=np.float32)
    for ci in range(NCH):
        oh_np[:, ci * 4 + ci] = 1.0
    onehot_h = nc.inline_tensor(oh_np, "onehot4")
    u_np = np.zeros((128, 128), dtype=bfnp)
    for blk_i in range(2):
        for tp in range(64):
            for t in range(tp, 64):
                u_np[blk_i * 64 + tp, blk_i * 64 + t] = 1
    ublk_h = nc.inline_tensor(u_np, "ublk")

    with contextlib.ExitStack() as st:
        e = st.enter_context

        W = {nm: e(nc.sbuf_tensor(nm + "_sb", [128, NJ * D], BF16))
             for nm in ("wk", "wv", "wq", "wkp", "wqp")}
        woeff = e(nc.sbuf_tensor("woeff", [128, NJ * D], BF16))
        exp_sb = e(nc.sbuf_tensor("exp_sb", [128, NJ * D], BF16))

        xnT = e(nc.sbuf_tensor("xnT", [128, NJ * TILE], BF16))
        keyT = e(nc.sbuf_tensor("keyT", [128, NJ * TILE], BF16))
        qryT = e(nc.sbuf_tensor("qryT", [128, NJ * TILE], BF16))

        xload = e(nc.sbuf_tensor("xload", [128, D], F32))
        xn_c = e(nc.sbuf_tensor("xn_c", [128, D], F32))
        xnbf_c = e(nc.sbuf_tensor("xnbf_c", [128, D], BF16))
        sc = [e(nc.sbuf_tensor(f"sc{i}", [128, D], F32)) for i in range(6)]
        cosb = e(nc.sbuf_tensor("cosb", [128, D], F32))
        sinb = e(nc.sbuf_tensor("sinb", [128, D], F32))
        br_sb = e(nc.sbuf_tensor("br_sb", [128, D], BF16))
        bi_sb = e(nc.sbuf_tensor("bi_sb", [128, D], BF16))
        z2bf = e(nc.sbuf_tensor("z2bf", [128, D], BF16))
        zT = e(nc.sbuf_tensor("zT", [128, D], BF16))
        psB = e(nc.sbuf_tensor("psB", [128, D], F32))

        cols = e(nc.sbuf_tensor("cols", [128, 16], F32))
        coli = e(nc.sbuf_tensor("coli", [128, 2], I32))
        mu_t = e(nc.sbuf_tensor("mu_t", [128, NCH], F32))
        rs_t = e(nc.sbuf_tensor("rs_t", [128, NCH], F32))

        ce8 = e(nc.sbuf_tensor("ce8", [8, D], F32))
        ceT = e(nc.sbuf_tensor("ceT", [128, 8 * NJ], F32))
        gate = e(nc.sbuf_tensor("gate", [8, 64], F32))
        e1row = e(nc.sbuf_tensor("e1row", [1, 8], F32))
        e2row = e(nc.sbuf_tensor("e2row", [1, 8], F32))
        idxp = e(nc.sbuf_tensor("idxp", [128, 8], I32))
        ebc = e(nc.sbuf_tensor("ebc", [128, 16], F32))

        H_sb = e(nc.sbuf_tensor("H_sb", [128, 48], F32))
        bkq_col = e(nc.sbuf_tensor("bkq_col", [128, 16], F32))
        lngb_col = e(nc.sbuf_tensor("lngb_col", [128, 16], F32))
        rows2 = e(nc.sbuf_tensor("rows2", [33, 4 * D], BF16))

        ident_bf = e(nc.sbuf_tensor("ident_bf", [128, 128], BF16))
        ident8 = e(nc.sbuf_tensor("ident8", [8, 8], F32))
        ublk = e(nc.sbuf_tensor("ublk_sb", [128, 128], BF16))
        iota_col = e(nc.sbuf_tensor("iota_sb", [128, 1], F32))
        pw8 = e(nc.sbuf_tensor("pw8_sb", [8, 6], F32))
        iota6 = e(nc.sbuf_tensor("iota6_sb", [8, 6], F32))
        iota6b = e(nc.sbuf_tensor("iota6b_sb", [8, 6], F32))
        onehot = e(nc.sbuf_tensor("onehot_sb", [128, 4 * NCH], F32))
        ones_col = e(nc.sbuf_tensor("ones_col", [128, 1], F32))
        ones1f = e(nc.sbuf_tensor("ones1f", [1, 128], F32))
        ones1b = e(nc.sbuf_tensor("ones1b", [1, 128], BF16))
        ones1c = e(nc.sbuf_tensor("ones1c", [33, 128], BF16))
        halfpi = e(nc.sbuf_tensor("halfpi", [128, 1], F32))

        big1 = e(nc.psum_tensor("big1", [128, D], F32))
        big2 = e(nc.psum_tensor("big2", [128, D], F32))
        feat = e(nc.psum_tensor("feat", [128, 512], F32))
        tr_bf = e(nc.psum_tensor("tr_bf", [128, 128], BF16))
        tr_f = e(nc.psum_tensor("tr_f", [128, 64], F32))

        sem = e(nc.semaphore("sq"))
        blk = e(nc.Block())
        sq = Seq(nc, sem)

        ROW_BV, ROW_BKP, ROW_BQP, ROW_BO = 0, 1, 2, 3

        sq.do("scalar", lambda s: s.preload_act_table(AF.Silu))
        for dst, src in ((ident_bf, ident_bf_h), (ident8, ident8_h), (ublk, ublk_h),
                         (iota_col, iota_col_h), (pw8, pw8_h), (iota6, iota6_h),
                         (iota6b, iota6b_h), (onehot, onehot_h)):
            sq.dma("sync", lambda g, dst=dst, src=src: g.dma_start(out=dst[:], in_=src.ap()[:]))
        sq.do("vector", lambda v: v.memset(ones_col[:], 1.0))
        sq.do("vector", lambda v: v.memset(ones1f[:], 1.0))
        sq.do("vector", lambda v: v.memset(ones1b[:], 1.0))
        sq.do("vector", lambda v: v.memset(ones1c[32:33, :], 1.0))
        sq.do("vector", lambda v: v.memset(ce8[:], 0.0))
        sq.do("vector", lambda v: v.memset(halfpi[:], math.pi / 2))

        for nm in ("wk", "wv", "wq"):
            sq.dma("gpsimd", lambda g, nm=nm: g.dma_start(
                out=W[nm][:], in_=wT[nm].rearrange("(p j) e -> p j e", j=NJ)))
        for nm in ("wkp", "wqp"):
            sq.dma("gpsimd", lambda g, nm=nm: g.dma_start(
                out=W[nm][:], in_=wT[nm].rearrange("(ec p) f -> p ec f", p=128)))

        sq.dma("sync", lambda g: g.dma_start(
            out=H_sb[:].rearrange("p (dc j) -> p dc j", j=6),
            in_=hyp_h.rearrange("(dc p) j -> p dc j", p=128)))
        sq.dma("sync", lambda g: g.dma_start(
            out=bkq_col[:, 0:8], in_=vec["bk"].rearrange("(ec p) -> p ec", p=128),
            allow_slow_non_contiguous=True))
        sq.dma("sync", lambda g: g.dma_start(
            out=bkq_col[:, 8:16], in_=vec["bq"].rearrange("(ec p) -> p ec", p=128),
            allow_slow_non_contiguous=True))
        sq.dma("sync", lambda g: g.dma_start(
            out=lngb_col[:, 0:8], in_=vec["lng"].rearrange("(p j) -> p j", j=NJ)))
        sq.dma("sync", lambda g: g.dma_start(
            out=lngb_col[:, 8:16], in_=vec["lnb"].rearrange("(p j) -> p j", j=NJ)))
        rtmp = sc[1][0:1, :]
        rtmp2 = sc[2][0:1, :]
        sq.dma("sync", lambda g: g.dma_start(
            out=rtmp, in_=vec["ps"].rearrange("(a d) -> a d", a=1)))
        sq.do("vector", lambda v: v.tensor_scalar(
            out=rtmp, in0=rtmp, scalar1=1.0 / TWO_PI, scalar2=None, op0=ALU.mult))
        for h in range(2):
            sq.do("tensor", lambda t, h=h: t.matmul(
                big1[:, h * 512:(h + 1) * 512], ones1f[:], sc[1][0:1, h * 512:(h + 1) * 512],
                start=True, stop=True))
        sq.do("vector", lambda v: v.tensor_copy(out=psB[:], in_=big1[:]))

        stage = sc[0]
        woT_r = wT["wo"].rearrange("(p j) e -> p j e", j=NJ)
        for j in range(NJ):
            sq.dma("sync", lambda g, j=j: g.dma_start(out=stage[:], in_=woT_r[:, j, :]))
            for h in range(2):
                sq.do("tensor", lambda t, j=j, h=h: t.matmul(
                    big2[0:1, h * 512:(h + 1) * 512], lngb_col[:, 8 + j:9 + j],
                    stage[:, h * 512:(h + 1) * 512], start=(j == 0), stop=(j == NJ - 1)))
            sq.do("vector", lambda v, j=j: v.tensor_scalar(
                out=woeff[:, j * D:(j + 1) * D], in0=stage[:],
                scalar1=lngb_col[:, j:j + 1], scalar2=None, op0=ALU.mult))
        for k, nm in enumerate(("bv", "bkp", "bqp", "bo")):
            sq.dma("sync", lambda g, nm=nm: g.dma_start(
                out=rtmp, in_=vec[nm].rearrange("(a d) -> a d", a=1)))
            if nm == "bo":
                sq.do("vector", lambda v: v.tensor_add(
                    out=rtmp2, in0=big2[0:1, :], in1=rtmp))
                sq.do("vector", lambda v: v.tensor_copy(out=rtmp, in_=rtmp2))
            sq.do("vector", lambda v, k=k: v.tensor_copy(
                out=rows2[0:1, k * D:(k + 1) * D], in_=rtmp))
            sq.do("vector", lambda v, k=k: v.tensor_sub(
                out=rtmp2, in0=rtmp, in1=rows2[0:1, k * D:(k + 1) * D]))
            sq.do("vector", lambda v, k=k: v.tensor_copy(
                out=rows2[32:33, k * D:(k + 1) * D], in_=rtmp2))

        def newton_rsqrt(vsl, ysl, t1sl, isl):
            sq.do("vector", lambda v: v.tensor_scalar(
                out=isl, in0=vsl.bitcast(I32), scalar1=1, scalar2=None,
                op0=ALU.arith_shift_right))
            sq.do("vector", lambda v: v.tensor_scalar(
                out=isl, in0=isl, scalar1=-1, scalar2=0x5F3759DF, op0=ALU.mult, op1=ALU.add))
            sq.do("vector", lambda v: v.tensor_copy(out=ysl, in_=isl.bitcast(F32)))
            for _ in range(3):
                sq.do("vector", lambda v: v.tensor_mul(out=t1sl, in0=vsl, in1=ysl))
                sq.do("vector", lambda v: v.tensor_mul(out=t1sl, in0=t1sl, in1=ysl))
                sq.do("vector", lambda v: v.tensor_scalar(
                    out=t1sl, in0=t1sl, scalar1=-0.5, scalar2=1.5, op0=ALU.mult, op1=ALU.add))
                sq.do("vector", lambda v: v.tensor_mul(out=ysl, in0=ysl, in1=t1sl))

        def bias_rows_mm(psum, rowk, stop):
            for h in range(2):
                sq.do("tensor", lambda t, h=h, rowk=rowk: t.matmul(
                    psum[:, h * 512:(h + 1) * 512], ones1b[:],
                    rows2[0:1, rowk * D + h * 512: rowk * D + (h + 1) * 512],
                    start=False, stop=False))
                sq.do("tensor", lambda t, h=h, rowk=rowk, stop=stop: t.matmul(
                    psum[:, h * 512:(h + 1) * 512], ones1c[32:33, :],
                    rows2[32:33, rowk * D + h * 512: rowk * D + (h + 1) * 512],
                    start=False, stop=stop))

        def phase(ci, srcT, wnm, rowk):
            for ec in range(8):
                for h in range(2):
                    sq.do("tensor", lambda t, ec=ec, h=h, ci=ci, srcT=srcT, wnm=wnm: t.matmul(
                        big2[:, h * 512:(h + 1) * 512],
                        srcT[:, ec * TILE + ci * 128: ec * TILE + ci * 128 + 128],
                        W[wnm][:, ec * D + h * 512: ec * D + (h + 1) * 512],
                        start=(ec == 0), stop=False))
            bias_rows_mm(big2, rowk, stop=True)
            sq.do("scalar", lambda s: s.activation(out=sc[0][:], in_=big2[:], func=AF.Tanh))
            sq.do("vector", lambda v: v.tensor_mul(out=sc[1][:], in0=sc[0][:], in1=psB[:]))
            sq.do("vector", lambda v: v.tensor_scalar(
                out=sc[2][:], in0=sc[1][:], scalar1=MAGIC, scalar2=MAGIC,
                op0=ALU.add, op1=ALU.subtract))
            sq.do("vector", lambda v: v.tensor_sub(out=sc[3][:], in0=sc[1][:], in1=sc[2][:]))
            sq.do("vector", lambda v: v.scalar_tensor_tensor(
                out=sc[2][:], in0=sc[3][:], scalar=-1.0, in1=sc[3][:],
                op0=ALU.mult, op1=ALU.max))
            sq.do("scalar", lambda s: s.activation(
                out=sinb[:], in_=sc[3][:], func=AF.Sin, scale=TWO_PI))
            sq.do("scalar", lambda s: s.activation(
                out=cosb[:], in_=sc[2][:], func=AF.Sin, bias=halfpi[:], scale=-TWO_PI))

        for ti in range(NTILE):
            t0 = ti * TILE

            for ci in range(NCH):
                r0 = t0 + ci * 128
                sq.dma("sync", lambda g, r0=r0: g.dma_start(
                    out=xload[:], in_=x_h[r0:r0 + 128, :]))
                sq.do("vector", lambda v: v.tensor_reduce(
                    out=cols[:, 0:1], in_=xload[:], axis=AX.X, op=ALU.add))
                sq.do("scalar", lambda s: s.activation(
                    out=sc[0][:], in_=xload[:], func=AF.Square, accum_out=cols[:, 1:2]))
                sq.do("vector", lambda v: v.tensor_scalar(
                    out=cols[:, 2:3], in0=cols[:, 0:1], scalar1=1.0 / D, scalar2=None,
                    op0=ALU.mult))
                sq.do("vector", lambda v: v.tensor_scalar(
                    out=cols[:, 3:4], in0=cols[:, 1:2], scalar1=1.0 / D, scalar2=None,
                    op0=ALU.mult))
                sq.do("vector", lambda v: v.scalar_tensor_tensor(
                    out=cols[:, 4:5], in0=cols[:, 2:3], scalar=cols[:, 2:3], in1=cols[:, 3:4],
                    op0=ALU.mult, op1=ALU.subtract))
                sq.do("vector", lambda v: v.tensor_scalar(
                    out=cols[:, 4:5], in0=cols[:, 4:5], scalar1=-1.0, scalar2=1e-5,
                    op0=ALU.mult, op1=ALU.add))
                newton_rsqrt(cols[:, 4:5], cols[:, 5:6], cols[:, 6:7], coli[:, 0:1])
                sq.do("vector", lambda v, ci=ci: v.tensor_copy(
                    out=mu_t[:, ci:ci + 1], in_=cols[:, 2:3]))
                sq.do("vector", lambda v, ci=ci: v.tensor_copy(
                    out=rs_t[:, ci:ci + 1], in_=cols[:, 5:6]))
                sq.do("vector", lambda v: v.tensor_scalar(
                    out=xn_c[:], in0=xload[:], scalar1=cols[:, 2:3], scalar2=cols[:, 5:6],
                    op0=ALU.subtract, op1=ALU.mult))
                sq.do("vector", lambda v: v.tensor_copy(out=xnbf_c[:], in_=xn_c[:]))
                for j in range(NJ):
                    sq.do("tensor", lambda t, j=j: t.transpose(
                        tr_bf[:], xnbf_c[:, j::NJ], ident_bf[:]))
                    sq.do("vector", lambda v, j=j, ci=ci: v.tensor_copy(
                        out=xnT[:, j * TILE + ci * 128: j * TILE + ci * 128 + 128],
                        in_=tr_bf[:]))
                for h in range(2):
                    sq.do("tensor", lambda t, h=h, ci=ci: t.matmul(
                        big2[0:NCH, h * 512:(h + 1) * 512], onehot[:, ci * 4:(ci + 1) * 4],
                        xn_c[:, h * 512:(h + 1) * 512],
                        start=(ci == 0), stop=(ci == NCH - 1)))
            sq.do("vector", lambda v: v.tensor_copy(out=ce8[0:NCH, :], in_=big2[0:NCH, :]))

            for dc in range(NJ):
                sq.do("tensor", lambda t, dc=dc: t.transpose(
                    tr_f[:, 0:8], ce8[:, dc * 128:(dc + 1) * 128], ident8[:]))
                sq.do("vector", lambda v, dc=dc: v.tensor_copy(
                    out=ceT[:, dc * 8:(dc + 1) * 8], in_=tr_f[:, 0:8]))
            for dc in range(NJ):
                sq.do("tensor", lambda t, dc=dc: t.matmul(
                    tr_f[0:8, 24:30], ceT[:, dc * 8:(dc + 1) * 8],
                    H_sb[:].rearrange("p (dc j) -> p dc j", j=6)[:, dc, :],
                    start=(dc == 0), stop=(dc == NJ - 1)))
            g8 = gate
            sq.do("vector", lambda v: v.tensor_copy(out=g8[:, 0:6], in_=tr_f[0:8, 24:30]))
            sq.do("vector", lambda v: v.tensor_scalar(
                out=g8[:, 8:14], in0=g8[:, 0:6], scalar1=0.0, scalar2=None, op0=ALU.is_gt))
            sq.do("vector", lambda v: v.tensor_mul(out=g8[:, 16:22], in0=g8[:, 8:14], in1=pw8[:]))
            sq.do("vector", lambda v: v.tensor_reduce(
                out=g8[:, 24:25], in_=g8[:, 16:22], axis=AX.X, op=ALU.add))
            sq.do("vector", lambda v: v.scalar_tensor_tensor(
                out=g8[:, 16:22], in0=g8[:, 0:6], scalar=-1.0, in1=g8[:, 0:6],
                op0=ALU.mult, op1=ALU.max))
            sq.do("vector", lambda v: v.tensor_reduce(
                out=g8[:, 25:26], in_=g8[:, 16:22], axis=AX.X, op=ALU.min))
            sq.do("vector", lambda v: v.tensor_scalar(
                out=g8[:, 32:38], in0=g8[:, 16:22], scalar1=g8[:, 25:26], scalar2=None,
                op0=ALU.is_le))
            sq.do("vector", lambda v: v.tensor_mul(out=g8[:, 40:46], in0=g8[:, 32:38], in1=iota6b[:]))
            sq.do("vector", lambda v: v.tensor_scalar(
                out=g8[:, 40:46], in0=g8[:, 40:46], scalar1=1.0e6, scalar2=None, op0=ALU.add))
            sq.do("vector", lambda v: v.tensor_reduce(
                out=g8[:, 26:27], in_=g8[:, 40:46], axis=AX.X, op=ALU.min))
            sq.do("vector", lambda v: v.tensor_scalar(
                out=g8[:, 32:38], in0=iota6[:], scalar1=g8[:, 26:27], scalar2=None,
                op0=ALU.is_equal))
            sq.do("vector", lambda v: v.tensor_mul(out=g8[:, 40:46], in0=g8[:, 32:38], in1=pw8[:]))
            sq.do("vector", lambda v: v.tensor_reduce(
                out=g8[:, 27:28], in_=g8[:, 40:46], axis=AX.X, op=ALU.add))
            sq.do("vector", lambda v: v.tensor_mul(out=g8[:, 40:46], in0=g8[:, 32:38], in1=g8[:, 8:14]))
            sq.do("vector", lambda v: v.tensor_reduce(
                out=g8[:, 28:29], in_=g8[:, 40:46], axis=AX.X, op=ALU.add))
            sq.do("vector", lambda v: v.tensor_scalar(
                out=g8[:, 29:30], in0=g8[:, 28:29], scalar1=-2.0, scalar2=1.0,
                op0=ALU.mult, op1=ALU.add))
            sq.do("vector", lambda v: v.tensor_mul(out=g8[:, 29:30], in0=g8[:, 29:30], in1=g8[:, 27:28]))
            sq.do("vector", lambda v: v.tensor_add(out=g8[:, 30:31], in0=g8[:, 24:25], in1=g8[:, 29:30]))
            sq.do("tensor", lambda t: t.transpose(tr_f[0:1, 32:40], g8[:, 24:25], ident8[:]))
            sq.do("vector", lambda v: v.tensor_copy(out=e1row[:], in_=tr_f[0:1, 32:40]))
            sq.do("tensor", lambda t: t.transpose(tr_f[0:1, 32:40], g8[:, 30:31], ident8[:]))
            sq.do("vector", lambda v: v.tensor_copy(out=e2row[:], in_=tr_f[0:1, 32:40]))
            sq.do("tensor", lambda t: t.matmul(
                tr_f[:, 8:16], ones1f[:], e1row[:], start=True, stop=True))
            sq.do("tensor", lambda t: t.matmul(
                tr_f[:, 16:24], ones1f[:], e2row[:], start=True, stop=True))
            sq.do("vector", lambda v: v.tensor_copy(out=ebc[:, 0:8], in_=tr_f[:, 8:16]))
            sq.do("vector", lambda v: v.tensor_copy(out=ebc[:, 8:16], in_=tr_f[:, 16:24]))
            for ci in range(NCH):
                for slot in range(2):
                    sq.do("vector", lambda v, ci=ci, slot=slot: v.scalar_tensor_tensor(
                        out=idxp[:, ci * 2 + slot:ci * 2 + slot + 1],
                        in0=ebc[:, slot * 8 + ci:slot * 8 + ci + 1], scalar=128.0,
                        in1=iota_col[:], op0=ALU.mult, op1=ALU.add))

            for dst, wnm, bc0 in ((keyT, "wk", 0), (qryT, "wq", 8)):
                for ec in range(8):
                    for j in range(NJ):
                        sq.do("tensor", lambda t, ec=ec, j=j, wnm=wnm: t.matmul(
                            feat[:, 0:512],
                            W[wnm][:, j * D + ec * 128: j * D + ec * 128 + 128],
                            xnT[:, j * TILE:(j + 1) * TILE],
                            start=(j == 0), stop=(j == NJ - 1)))
                    sq.do("scalar", lambda s, ec=ec, dst=dst, bc0=bc0: s.activation(
                        out=dst[:, ec * TILE:(ec + 1) * TILE], in_=feat[:, 0:512],
                        func=AF.Identity, bias=bkq_col[:, bc0 + ec:bc0 + ec + 1], scale=1.0))

            for ci in range(NCH):
                r0 = t0 + ci * 128

                for j in range(NJ):
                    for h in range(2):
                        sq.do("tensor", lambda t, j=j, h=h, ci=ci: t.matmul(
                            big1[:, h * 512:(h + 1) * 512],
                            xnT[:, j * TILE + ci * 128: j * TILE + ci * 128 + 128],
                            W["wv"][:, j * D + h * 512: j * D + (h + 1) * 512],
                            start=(j == 0), stop=False))
                bias_rows_mm(big1, ROW_BV, stop=True)
                sq.do("scalar", lambda s: s.activation(
                    out=sc[5][:], in_=big1[:], func=AF.Copy, scale=1.0))

                phase(ci, keyT, "wkp", ROW_BKP)
                sq.do("vector", lambda v: v.tensor_mul(out=br_sb[:], in0=sc[5][:], in1=cosb[:]))
                sq.do("vector", lambda v: v.tensor_mul(out=bi_sb[:], in0=sc[5][:], in1=sinb[:]))
                for h in range(2):
                    sq.do("tensor", lambda t, h=h: t.matmul(
                        big1[:, h * 512:(h + 1) * 512], ublk[:], br_sb[:, h * 512:(h + 1) * 512],
                        start=True, stop=True))
                phase(ci, qryT, "wqp", ROW_BQP)
                for h in range(2):
                    sq.do("tensor", lambda t, h=h: t.matmul(
                        big2[:, h * 512:(h + 1) * 512], ublk[:], bi_sb[:, h * 512:(h + 1) * 512],
                        start=True, stop=True))
                sq.do("vector", lambda v: v.tensor_mul(out=sc[1][:], in0=big1[:], in1=cosb[:]))
                sq.do("vector", lambda v: v.tensor_mul(out=sc[2][:], in0=big2[:], in1=sinb[:]))
                sq.do("vector", lambda v: v.scalar_tensor_tensor(
                    out=sc[3][:], in0=sc[1][:], scalar=1.0, in1=sc[2][:],
                    op0=ALU.mult, op1=ALU.add, accum_out=cols[:, 8:9]))
                sq.do("scalar", lambda s: s.activation(
                    out=sc[1][:], in_=sc[3][:], func=AF.Square, accum_out=cols[:, 9:10]))
                sq.do("vector", lambda v: v.tensor_scalar(
                    out=cols[:, 10:11], in0=cols[:, 8:9], scalar1=1.0 / D, scalar2=None,
                    op0=ALU.mult))
                sq.do("vector", lambda v: v.tensor_scalar(
                    out=cols[:, 11:12], in0=cols[:, 9:10], scalar1=1.0 / D, scalar2=None,
                    op0=ALU.mult))
                sq.do("vector", lambda v: v.scalar_tensor_tensor(
                    out=cols[:, 12:13], in0=cols[:, 10:11], scalar=cols[:, 10:11],
                    in1=cols[:, 11:12], op0=ALU.mult, op1=ALU.subtract))
                sq.do("vector", lambda v: v.tensor_scalar(
                    out=cols[:, 12:13], in0=cols[:, 12:13], scalar1=-1.0 / 1024.0,
                    scalar2=1e-5, op0=ALU.mult, op1=ALU.add))
                newton_rsqrt(cols[:, 12:13], cols[:, 13:14], cols[:, 6:7], coli[:, 0:1])
                sq.do("vector", lambda v: v.tensor_scalar(
                    out=cols[:, 13:14], in0=cols[:, 13:14], scalar1=1.0 / 32.0, scalar2=None,
                    op0=ALU.mult))
                sq.do("vector", lambda v: v.tensor_scalar(
                    out=z2bf[:], in0=sc[3][:], scalar1=cols[:, 10:11], scalar2=cols[:, 13:14],
                    op0=ALU.subtract, op1=ALU.mult))
                for j in range(NJ):
                    sq.do("tensor", lambda t, j=j: t.transpose(
                        tr_bf[:], z2bf[:, j::NJ], ident_bf[:]))
                    sq.do("vector", lambda v, j=j: v.tensor_copy(
                        out=zT[:, j * 128:(j + 1) * 128], in_=tr_bf[:]))
                for j in range(NJ):
                    for h in range(2):
                        sq.do("tensor", lambda t, j=j, h=h: t.matmul(
                            big1[:, h * 512:(h + 1) * 512], zT[:, j * 128:(j + 1) * 128],
                            woeff[:, j * D + h * 512: j * D + (h + 1) * 512],
                            start=(j == 0), stop=False))
                bias_rows_mm(big1, ROW_BO, stop=True)
                sq.dma("sync", lambda g, r0=r0: g.dma_start(
                    out=xload[:], in_=x_h[r0:r0 + 128, :]))
                sq.do("vector", lambda v, ci=ci: v.tensor_scalar(
                    out=xn_c[:], in0=xload[:], scalar1=mu_t[:, ci:ci + 1],
                    scalar2=rs_t[:, ci:ci + 1], op0=ALU.subtract, op1=ALU.mult))
                sq.do("vector", lambda v: v.tensor_add(out=sc[4][:], in0=xn_c[:], in1=big1[:]))

                for slot in range(2):
                    sq.dma("gpsimd", lambda g, ci=ci, slot=slot: g.indirect_dma_start(
                        out=exp_sb[:], out_offset=None, in_=ew_rows,
                        in_offset=IndirectOffsetOnAxis(
                            idxp[:, ci * 2 + slot:ci * 2 + slot + 1], 0),
                        element_offset=0))
                    for j in range(NJ):
                        for h in range(2):
                            sq.do("tensor", lambda t, j=j, h=h, ci=ci, slot=slot: t.matmul(
                                big2[:, h * 512:(h + 1) * 512],
                                xnT[:, j * TILE + ci * 128: j * TILE + ci * 128 + 128],
                                exp_sb[:, j * D + h * 512: j * D + (h + 1) * 512],
                                start=(slot == 0 and j == 0),
                                stop=(slot == 1 and j == NJ - 1)))
                sq.do("scalar", lambda s: s.activation(
                    out=sc[1][:], in_=big2[:], func=AF.Silu, scale=0.5))
                sq.do("vector", lambda v: v.tensor_mul(out=sc[2][:], in0=sc[4][:], in1=sc[1][:]))
                sq.do("vector", lambda v: v.tensor_add(out=sc[3][:], in0=xload[:], in1=sc[2][:]))
                sq.dma("sync", lambda g, r0=r0: g.dma_start(
                    out=out_h[r0:r0 + 128, :], in_=sc[3][:]))

        sq.emit(blk)
    return nc


_NC_CACHE = {}


def kernel(**inputs):
    x = np.ascontiguousarray(np.asarray(inputs["L0_attn_x"], np.float32))
    B = x.shape[0]
    wmap = {
        "wkT": "L0_attn_Wk", "wvT": "L0_attn_Wv", "wqT": "L0_attn_Wq",
        "wkpT": "L0_attn_Wkp", "wqpT": "L0_attn_Wqp", "woT": "L0_attn_Wo",
    }
    vmap = {
        "bk": "L0_attn_bk", "bv": "L0_attn_bv", "bq": "L0_attn_bq",
        "bkp": "L0_attn_bkp", "bqp": "L0_attn_bqp", "ps": "L0_attn_phase_scale",
        "lng": "L0_attn_ln_g", "lnb": "L0_attn_ln_b", "bo": "L0_attn_bo",
    }
    shared = {}
    for k, src in wmap.items():
        shared[k] = np.ascontiguousarray(np.asarray(inputs[src], np.float32).T)
    for k, src in vmap.items():
        shared[k] = np.ascontiguousarray(np.asarray(inputs[src], np.float32))
    shared["hyp"] = np.ascontiguousarray(np.asarray(inputs["L0_ffn_hyperplanes"], np.float32))
    shared["ew"] = np.ascontiguousarray(np.asarray(inputs["L0_ffn_expert_weights"], np.float32))

    if "nc" not in _NC_CACHE:
        _NC_CACHE["nc"] = build_program()
    nc = _NC_CACHE["nc"]

    in_maps = []
    for b in range(B):
        m = dict(shared)
        m["x"] = np.ascontiguousarray(x[b])
        in_maps.append(m)
    res = run_bass_kernel_spmd(nc, in_maps, core_ids=list(range(B)))
    out = np.stack([np.asarray(r["out"], np.float32) for r in res.results], axis=0)
    return out


# version 04
# speedup vs baseline: 1.9385x; 1.9385x over previous
import math
import os
import contextlib
import numpy as np
import ml_dtypes
import concourse.bass as bass
import concourse.mybir as mybir
from concourse.bass_utils import run_bass_kernel_spmd
from concourse.bass_core_rust import IndirectOffsetOnAxis

F32 = mybir.dt.float32
BF16 = mybir.dt.bfloat16
I32 = mybir.dt.int32
AF = mybir.ActivationFunctionType
ALU = mybir.AluOpType
AX = mybir.AxisListType
bfnp = ml_dtypes.bfloat16

D = 1024
S = 4096
NJ = 8
TILE = 512
NTILE = S // TILE
NCH = TILE // 128
NEXP = 3
MAGIC = 1.5 * 2.0 ** 23
TWO_PI = 2.0 * math.pi
ENGS = ("sync", "gpsimd", "scalar", "vector", "tensor")


class Flow:

    def __init__(self, nc):
        self.nc = nc
        self.ops = {e: [] for e in ENGS}
        self.chan_cnt = {}
        self.last_w = {}
        self.readers = {}
        self.synced = {e: {} for e in ENGS}
        self.gseq = []

    SERIAL = False
    GLOBALSEM = False
    STRICT_ENGINES = ("scalar", "vector")

    CHANPOOL = 0

    def do(self, eng, fn, reads=(), writes=(), inc=1, channel=None):
        import sys
        self._lines = getattr(self, "_lines", {e: [] for e in ENGS})
        f = sys._getframe(1)
        if f.f_code.co_name == "dma":
            f = sys._getframe(2)
        self._lines[eng].append(f.f_lineno)
        self.gseq.append((eng, len(self.ops[eng]), inc))
        if Flow.SERIAL:
            reads = tuple(reads)
            writes = tuple(writes) + ("GLOB",)
        is_dma = inc == 16
        if is_dma:
            chan = channel if channel is not None else (writes[0] if writes else "misc")
            if Flow.CHANPOOL:
                chan = "SY" if eng == "sync" else "GP"
            n = self.chan_cnt.get(chan, 0) + 1
            self.chan_cnt[chan] = n
            ref = ("c", chan, n)
        else:
            ref = ("e", eng, len(self.ops[eng]))
        need = {}

        def add(p):
            if (p[0] == "e" and p[1] == eng
                    and eng not in Flow.STRICT_ENGINES):
                return
            k = (p[0], p[1])
            if need.get(k, (-1,))[0] < p[2]:
                need[k] = (p[2], p)

        for r in reads:
            w = self.last_w.get(r)
            if w:
                add(w)
        for r in writes:
            w = self.last_w.get(r)
            if w:
                add(w)
            for p in self.readers.get(r, ()):
                add(p)
        if is_dma and n > 1:
            add(("c", chan, n - 1))
        deps = []
        for k, (val, p) in need.items():
            if self.synced[eng].get(k, -1) < val:
                deps.append(p)
                self.synced[eng][k] = val
        self.ops[eng].append([fn, deps, ref])
        for r in reads:
            self.readers.setdefault(r, []).append(ref)
        for r in writes:
            self.last_w[r] = ref
            self.readers[r] = []
        return ref

    def dma(self, eng, fn, reads=(), writes=(), channel=None):
        return self.do(eng, fn, reads=reads, writes=writes, inc=16, channel=channel)

    def emit(self, block):
        if Flow.GLOBALSEM:
            return self._emit_global(block)
        marked = {e: set() for e in ENGS}
        for e in ENGS:
            for fn, deps, ref in self.ops[e]:
                for p in deps:
                    if p[0] == "e":
                        marked[p[1]].add(p[2])
        semval = {}
        for e in ENGS:
            run = 0
            vals = []
            for i, (fn, deps, ref) in enumerate(self.ops[e]):
                if ref[0] == "e" and i in marked[e]:
                    run += 1
                vals.append(run)
            semval[e] = vals
        esem = {e: self.nc.alloc_semaphore(f"fe_{e}") for e in ENGS}
        csem = {c: self.nc.alloc_semaphore(f"fc_{i}")
                for i, c in enumerate(sorted(self.chan_cnt))}
        out_chans = [c for c in self.chan_cnt if c.startswith("OUT")]

        def wait_val(p):
            if p[0] == "e":
                return esem[p[1]], semval[p[1]][p[2]]
            return csem[p[1]], 16 * p[2]

        def mk(engname):
            def body(eng):
                for i, (fn, deps, ref) in enumerate(self.ops[engname]):
                    for p in deps:
                        s, v = wait_val(p)
                        eng.wait_ge(s, v)
                    inst = fn(eng)
                    if ref[0] == "c":
                        inst.then_inc(csem[ref[1]], 16)
                    elif i in marked[engname]:
                        inst.then_inc(esem[engname], 1)
                if engname == "sync":
                    for c in out_chans:
                        eng.wait_ge(csem[c], 16 * self.chan_cnt[c])
            return body

        for name in ENGS:
            if self.ops[name] or name == "sync":
                getattr(block, name)(mk(name))

    def _emit_global(self, block):
        mode = os.environ.get("KGSEMS", "1")
        if mode in ("25", "26", "3", "4", "45", "5"):
            return self._emit_global5(block)
        two = mode == "2"
        semc = self.nc.alloc_semaphore("gsemc")
        semd = self.nc.alloc_semaphore("gsemd") if two else semc
        waitv = {}
        runc, rund = 0, 0
        for eng, idx, inc in self.gseq:
            waitv[(eng, idx)] = (runc, rund)
            if inc == 16:
                rund += 16
            else:
                runc += 1
        finalc, finald = runc, rund

        def mk(engname):
            def body(eng):
                for i, (fn, deps, ref) in enumerate(self.ops[engname]):
                    wc, wd = waitv[(engname, i)]
                    if two:
                        if wc > 0:
                            eng.wait_ge(semc, wc)
                        if wd > 0:
                            eng.wait_ge(semd, wd)
                    else:
                        if wc + wd > 0:
                            eng.wait_ge(semc, wc + wd)
                    inc = 16 if ref[0] == "c" else 1
                    fn(eng).then_inc(semd if (two and inc == 16) else semc, inc)
                if engname == "sync":
                    eng.wait_ge(semc, finalc)
                    if two and finald:
                        eng.wait_ge(semd, finald)
            return body

        for name in ENGS:
            if self.ops[name] or name == "sync":
                getattr(block, name)(mk(name))
        return None

    def _emit_global5(self, block):
        mode = os.environ.get("KGSEMS", "5")
        if mode in ("25", "26"):
            group = {"sync": "D", "gpsimd": "D", "scalar": "C", "vector": "C", "tensor": "C"}
        elif mode == "3":
            group = {"sync": "SY", "gpsimd": "GP", "scalar": "C", "vector": "C", "tensor": "C"}
        elif mode == "4":
            group = {"sync": "D", "gpsimd": "D", "scalar": "scalar", "vector": "vector", "tensor": "tensor"}
        elif mode == "45":
            group = {"sync": "D", "gpsimd": "GP2", "scalar": "scalar", "vector": "vector", "tensor": "tensor"}
        else:
            group = {e: e for e in ENGS}
        gset = sorted(set(group.values()))
        gsem = {g: self.nc.alloc_semaphore(f"g5x{g}".replace("sync", "zp")) for g in gset}
        dummy = self.nc.alloc_semaphore("g5dummy") if mode == "26" else None
        sems = {e: gsem[group[e]] for e in ENGS}
        ENGG = gset
        run = {g: 0 for g in gset}
        waitv = {}
        for eng, idx, inc in self.gseq:
            waitv[(eng, idx)] = dict(run)
            run[group[eng]] += 16 if inc == 16 else 1
        final = dict(run)

        def mk(engname):
            def body(eng):
                g0 = group[engname]
                prev = {g: -1 for g in ENGG}
                for i, (fn, deps, ref) in enumerate(self.ops[engname]):
                    nw = 0
                    for o in ENGG:
                        v = waitv[(engname, i)][o]
                        if o != g0 and v > 0 and v > prev[o]:
                            eng.wait_ge(gsem[o], v)
                            prev[o] = v
                            nw += 1
                    if nw == 0 and dummy is not None:
                        eng.wait_ge(dummy, 0)
                    inc = 16 if ref[0] == "c" else 1
                    fn(eng).then_inc(gsem[g0], inc)
                if engname == "sync":
                    for o in ENGG:
                        if final[o]:
                            eng.wait_ge(gsem[o], final[o])
            return body

        for name in ENGS:
            if self.ops[name] or name == "sync":
                getattr(block, name)(mk(name))


def build_program(ntile=NTILE):
    kcut = int(os.environ.get("KCUT", "9"))
    Flow.SERIAL = os.environ.get("KSERIAL", "0") == "1"
    Flow.GLOBALSEM = os.environ.get("KGLOBAL", "0") == "1"
    Flow.CHANPOOL = int(os.environ.get("KCHANPOOL", "0"))
    ks = os.environ.get("KSTRICT", "sv")
    Flow.STRICT_ENGINES = {"s": ("scalar",), "all": ("scalar", "vector", "tensor", "gpsimd", "sync"),
                           "sv": ("scalar", "vector"), "st": ("scalar", "tensor"),
                           "svt": ("scalar", "vector", "tensor"),
                           "none": ()}[ks]
    nc = bass.Bass("TRN2", target_bir_lowering=False, detect_race_conditions=False)

    x_h = nc.dram_tensor("x", [S, D], F32, kind="ExternalInput").ap()
    out_h = nc.dram_tensor("out", [S, D], F32, kind="ExternalOutput").ap()
    wT = {}
    for nm in ("wv", "wkp", "wqp", "wo"):
        wT[nm] = nc.dram_tensor(nm + "T", [D, D], F32, kind="ExternalInput").ap()
    wg = {}
    for nm in ("wk", "wq"):
        wg[nm] = nc.dram_tensor(nm + "G", [D, D], F32, kind="ExternalInput").ap()
    vec = {}
    for nm in ("bk", "bv", "bq", "bkp", "bqp", "ps", "lng", "lnb", "bo"):
        vec[nm] = nc.dram_tensor(nm, [D], F32, kind="ExternalInput").ap()
    hyp_h = nc.dram_tensor("hyp", [D, 6], F32, kind="ExternalInput").ap()
    dbg_h = (nc.dram_tensor("dbg", [NTILE, 128, 80], F32, kind="ExternalOutput").ap()
             if os.environ.get("KDBG") == "1" else None)
    ew_h = nc.dram_tensor("ew", [64, D, D], F32, kind="ExternalInput").ap()
    ew_rows = ew_h.rearrange("e d k -> (e d k)").rearrange("(r c) -> r c", c=NJ * D)

    ident_bf_h = nc.inline_tensor(np.eye(128, dtype=bfnp), "identbf")
    ident8_h = nc.inline_tensor(np.eye(8, dtype=np.float32), "ident8f")
    iota_col_h = nc.inline_tensor(np.arange(128, dtype=np.float32).reshape(128, 1), "iotacol")
    pw8_h = nc.inline_tensor(np.tile((2.0 ** np.arange(6)).astype(np.float32), (8, 1)), "pw8")
    iota6_h = nc.inline_tensor(np.tile(np.arange(6, dtype=np.float32), (8, 1)), "iota6")
    iota6b_h = nc.inline_tensor(
        np.tile(np.arange(6, dtype=np.float32) - 1.0e6, (8, 1)), "iota6b")
    oh_np = np.zeros((128, 4 * NCH), dtype=np.float32)
    for ci in range(NCH):
        oh_np[:, ci * 4 + ci] = 1.0
    onehot_h = nc.inline_tensor(oh_np, "onehot4")
    u_np = np.zeros((128, 128), dtype=bfnp)
    for blk_i in range(2):
        for tp_ in range(64):
            for t_ in range(tp_, 64):
                u_np[blk_i * 64 + tp_, blk_i * 64 + t_] = 1
    ublk_h = nc.inline_tensor(u_np, "ublk")

    with contextlib.ExitStack() as st:
        e = st.enter_context

        W = {nm: e(nc.sbuf_tensor(nm + "_sb", [128, NJ * D], BF16))
             for nm in ("wv", "wck", "wcq", "wo")}
        exp_sb = [e(nc.sbuf_tensor(f"exp{k}", [128, NJ * D], BF16)) for k in range(NEXP)]

        xnT = [e(nc.sbuf_tensor(f"xnT{p}", [128, NJ * 128], BF16)) for p in range(2)]
        xloadA = [e(nc.sbuf_tensor("xlA0", [128, D], F32))] * 2
        xloadB = [e(nc.sbuf_tensor(f"xlB{p}", [128, D], F32)) for p in range(2)]
        xn_c = [e(nc.sbuf_tensor(f"xn_c{p}", [128, D], F32)) for p in range(2)]
        xnbf = [e(nc.sbuf_tensor(f"xnbf{p}", [128, D], BF16)) for p in range(2)]
        scA = e(nc.sbuf_tensor("scA", [128, D], F32))
        xn_b = scA
        sc = [e(nc.sbuf_tensor(f"sc{i}", [128, D], F32)) for i in range(4)]
        valb = e(nc.sbuf_tensor("valb", [128, D], F32))
        cosb = e(nc.sbuf_tensor("cosb", [128, D], F32))
        sinb = e(nc.sbuf_tensor("sinb", [128, D], F32))
        br_sb = e(nc.sbuf_tensor("br_sb", [128, D], BF16))
        bi_sb = e(nc.sbuf_tensor("bi_sb", [128, D], BF16))
        zT = e(nc.sbuf_tensor("zT", [128, D], BF16))
        psB = e(nc.sbuf_tensor("psB", [128, D], F32))

        colsA = [e(nc.sbuf_tensor(f"colsA{p}", [128, 8], F32)) for p in range(2)]
        colsB = e(nc.sbuf_tensor("colsB", [128, 8], F32))
        coliA = [e(nc.sbuf_tensor(f"coliA{p}", [128, 1], I32)) for p in range(2)]
        coliB = e(nc.sbuf_tensor("coliB", [128, 1], I32))
        mu_t = [e(nc.sbuf_tensor(f"mu_t{p}", [128, NCH], F32)) for p in range(2)]
        rs_t = [e(nc.sbuf_tensor(f"rs_t{p}", [128, NCH], F32)) for p in range(2)]

        ce8 = e(nc.sbuf_tensor("ce8", [8, D], F32))
        ceT = e(nc.sbuf_tensor("ceT", [128, 8 * NJ], F32))
        gate = e(nc.sbuf_tensor("gate", [8, 64], F32))
        e1row = e(nc.sbuf_tensor("e1row", [1, 8], F32))
        e2row = e(nc.sbuf_tensor("e2row", [1, 8], F32))
        idxp = e(nc.sbuf_tensor("idxp", [128, 8], I32))
        ebc = e(nc.sbuf_tensor("ebc", [128, 16], F32))

        H_sb = e(nc.sbuf_tensor("H_sb", [128, 48], F32))
        bkq_col = e(nc.sbuf_tensor("bkq_col", [128, 16], F32))
        bkq_bf = e(nc.sbuf_tensor("bkq_bf", [128, 32], BF16))
        lngb_col = e(nc.sbuf_tensor("lngb_col", [128, 16], F32))
        rows2 = e(nc.sbuf_tensor("rows2", [1, 4 * D], F32))

        ident_bf = e(nc.sbuf_tensor("ident_bf", [128, 128], BF16))
        ident8 = e(nc.sbuf_tensor("ident8", [8, 8], F32))
        ublk = e(nc.sbuf_tensor("ublk_sb", [128, 128], BF16))
        iota_col = e(nc.sbuf_tensor("iota_sb", [128, 1], F32))
        pw8 = e(nc.sbuf_tensor("pw8_sb", [8, 6], F32))
        iota6 = e(nc.sbuf_tensor("iota6_sb", [8, 6], F32))
        iota6b = e(nc.sbuf_tensor("iota6b_sb", [8, 6], F32))
        onehot = e(nc.sbuf_tensor("onehot_sb", [128, 4 * NCH], F32))
        ones1f = e(nc.sbuf_tensor("ones1f", [1, 128], F32))
        ones1b = e(nc.sbuf_tensor("ones1b", [1, 128], BF16))
        ones1c = e(nc.sbuf_tensor("ones1c", [33, 128], BF16))
        halfpi = e(nc.sbuf_tensor("halfpi", [128, 1], F32))

        bigA = e(nc.psum_tensor("bigA", [128, D], F32))
        bigB = e(nc.psum_tensor("bigB", [128, D], F32))
        bigC = e(nc.psum_tensor("bigC", [128, D], F32))
        tr_bf = e(nc.psum_tensor("tr_bf", [128, 128], BF16))
        tr_f = e(nc.psum_tensor("tr_f", [128, 64], F32))

        blk = e(nc.Block())
        fl = Flow(nc)
        bc_reg = nc.alloc_register(mybir.EngineType.Pool, "bcreg")

        ROW_BV, ROW_BK, ROW_BQ, ROW_BO = 0, 1, 2, 3

        fl.do("scalar", lambda s: s.preload_act_table(AF.Silu))
        for ki, (dst, src) in enumerate(((ident_bf, ident_bf_h), (ident8, ident8_h),
                                         (ublk, ublk_h), (iota_col, iota_col_h),
                                         (pw8, pw8_h), (iota6, iota6_h),
                                         (iota6b, iota6b_h), (onehot, onehot_h))):
            fl.dma("sync", lambda g, dst=dst, src=src: g.dma_start(out=dst[:], in_=src.ap()[:]),
                   writes=("consts",), channel=f"k{ki}")
        fl.do("vector", lambda v: v.memset(ones1f[:], 1.0), writes=("consts",))
        fl.do("vector", lambda v: v.memset(ones1b[:], 1.0), writes=("consts",))
        fl.do("vector", lambda v: v.memset(ones1c[32:33, :], 1.0), writes=("consts",))
        fl.do("vector", lambda v: v.memset(ce8[:], 0.0), writes=("ce8",))
        fl.do("vector", lambda v: v.memset(halfpi[:], math.pi / 2), writes=("consts",))

        fl.do("gpsimd", lambda g: g.reg_mov(bc_reg, NJ * D - 1))
        fl.dma("gpsimd", lambda g: g.dma_start(
            out=W["wv"][:], in_=wT["wv"].rearrange("(p j) e -> p j e", j=NJ)),
            writes=("Wv",))
        fl.dma("gpsimd", lambda g: g.dma_start(
            out=W["wo"][:], in_=wT["wo"].rearrange("(p j) e -> p j e", j=NJ)),
            writes=("Wo",))
        fl.dma("gpsimd", lambda g: g.dma_start(
            out=exp_sb[0][:], in_=wT["wkp"].rearrange("(ec p) f -> p ec f", p=128)),
            writes=("exp0",))
        fl.dma("gpsimd", lambda g: g.dma_start(
            out=exp_sb[1][:], in_=wT["wqp"].rearrange("(ec p) f -> p ec f", p=128)),
            writes=("exp1",))
        fl.dma("gpsimd", lambda g: g.dma_start(
            out=exp_sb[2][:], in_=wg["wk"].rearrange("(ec p) d -> p ec d", p=128)),
            writes=("exp2",))

        fl.dma("sync", lambda g: g.dma_start(
            out=H_sb[:].rearrange("p (dc j) -> p dc j", j=6),
            in_=hyp_h.rearrange("(dc p) j -> p dc j", p=128)), writes=("H",))
        fl.dma("sync", lambda g: g.dma_start(
            out=bkq_col[:, 0:8], in_=vec["bk"].rearrange("(ec p) -> p ec", p=128),
            allow_slow_non_contiguous=True), writes=("bkqc",), channel="bkq")
        fl.dma("sync", lambda g: g.dma_start(
            out=bkq_col[:, 8:16], in_=vec["bq"].rearrange("(ec p) -> p ec", p=128),
            allow_slow_non_contiguous=True), writes=("bkqc",), channel="bkq")
        fl.dma("sync", lambda g: g.dma_start(
            out=lngb_col[:, 0:8], in_=vec["lng"].rearrange("(p j) -> p j", j=NJ)),
            writes=("lngb",), channel="lngb")
        fl.dma("sync", lambda g: g.dma_start(
            out=lngb_col[:, 8:16], in_=vec["lnb"].rearrange("(p j) -> p j", j=NJ)),
            writes=("lngb",), channel="lngb")
        fl.do("vector", lambda v: v.tensor_copy(out=bkq_bf[:, 0:16], in_=bkq_col[:]),
              reads=("bkqc",), writes=("bkqbf",))
        fl.do("vector", lambda v: v.tensor_sub(
            out=scA[:, 0:16], in0=bkq_col[:], in1=bkq_bf[:, 0:16]),
            reads=("bkqc", "bkqbf"), writes=("scA",))
        fl.do("vector", lambda v: v.tensor_copy(out=bkq_bf[:, 16:32], in_=scA[:, 0:16]),
              reads=("scA",), writes=("bkqbf",))

        rtmp = sc[1][0:1, :]
        rtmp2 = sc[2][0:1, :]
        fl.dma("sync", lambda g: g.dma_start(
            out=rtmp, in_=vec["ps"].rearrange("(a d) -> a d", a=1)), writes=("sc1",))
        fl.do("vector", lambda v: v.tensor_scalar(
            out=rtmp, in0=rtmp, scalar1=1.0 / TWO_PI, scalar2=None, op0=ALU.mult),
            reads=("sc1",), writes=("sc1",))
        for h in range(2):
            fl.do("tensor", lambda t, h=h: t.matmul(
                bigA[:, h * 512:(h + 1) * 512], ones1f[:], sc[1][0:1, h * 512:(h + 1) * 512],
                start=True, stop=True), reads=("sc1", "consts"), writes=("bigA",))
        fl.do("vector", lambda v: v.tensor_copy(out=psB[:], in_=bigA[:]),
              reads=("bigA",), writes=("psB",))

        for dst, pexp in (("wck", 0), ("wcq", 1)):
            if dst == "wcq":
                fl.dma("gpsimd", lambda g: g.dma_start(
                    out=exp_sb[2][:], in_=wg["wq"].rearrange("(ec p) d -> p ec d", p=128)),
                    writes=("exp2",))
            for j in range(NJ):
                for h in range(2):
                    for ec in range(8):
                        fl.do("tensor", lambda t, j=j, h=h, ec=ec, pexp=pexp: t.matmul(
                            bigB[:, h * 512:(h + 1) * 512],
                            exp_sb[2][:, ec * D + j: (ec + 1) * D: NJ],
                            exp_sb[pexp][:, ec * D + h * 512: ec * D + (h + 1) * 512],
                            start=(ec == 0), stop=(ec == 7)),
                            reads=("exp2", f"exp{pexp}"), writes=("bigB",))
                    fl.do("vector", lambda v, j=j, h=h, dst=dst: v.tensor_copy(
                        out=W[dst][:, j * D + h * 512: j * D + (h + 1) * 512],
                        in_=bigB[:, h * 512:(h + 1) * 512]),
                        reads=("bigB",), writes=("W" + dst[1:],))
            bcol = 0 if dst == "wck" else 8
            for ec in range(8):
                for part, base in ((0, 0), (1, 16)):
                    for h in range(2):
                        fl.do("tensor",
                              lambda t, ec=ec, h=h, bcol=bcol, base=base, pexp=pexp,
                              part=part: t.matmul(
                                  bigC[0:1, h * 512:(h + 1) * 512],
                                  bkq_bf[:, base + bcol + ec: base + bcol + ec + 1],
                                  exp_sb[pexp][:, ec * D + h * 512: ec * D + (h + 1) * 512],
                                  start=(ec == 0 and part == 0),
                                  stop=(ec == 7 and part == 1)),
                              reads=("bkqbf", f"exp{pexp}"), writes=("bigC",))
            nm2 = "bkp" if dst == "wck" else "bqp"
            rowk = ROW_BK if dst == "wck" else ROW_BQ
            fl.dma("sync", lambda g, nm2=nm2: g.dma_start(
                out=rtmp, in_=vec[nm2].rearrange("(a d) -> a d", a=1)), writes=("sc1",))
            fl.do("vector", lambda v: v.tensor_add(out=rtmp2, in0=bigC[0:1, :], in1=rtmp),
                  reads=("bigC", "sc1"), writes=("sc2",))
            fl.do("vector", lambda v, rowk=rowk: v.tensor_copy(
                out=rows2[0:1, rowk * D:(rowk + 1) * D], in_=rtmp2),
                reads=("sc2",), writes=("rows",))

        woT_r = wT["wo"].rearrange("(p j) e -> p j e", j=NJ)
        for j in range(NJ):
            fl.dma("sync", lambda g, j=j: g.dma_start(out=scA[:], in_=woT_r[:, j, :]),
                   writes=("scA",))
            for h in range(2):
                fl.do("tensor", lambda t, j=j, h=h: t.matmul(
                    bigB[0:1, h * 512:(h + 1) * 512], lngb_col[:, 8 + j:9 + j],
                    scA[:, h * 512:(h + 1) * 512], start=(j == 0), stop=(j == NJ - 1)),
                    reads=("scA", "lngb"), writes=("bigB",))
        for k, nm in ((ROW_BV, "bv"), (ROW_BO, "bo")):
            fl.dma("sync", lambda g, nm=nm: g.dma_start(
                out=rtmp, in_=vec[nm].rearrange("(a d) -> a d", a=1)), writes=("sc1",))
            if nm == "bo":
                fl.do("vector", lambda v: v.tensor_add(out=rtmp2, in0=bigB[0:1, :], in1=rtmp),
                      reads=("bigB", "sc1"), writes=("sc2",))
                fl.do("vector", lambda v: v.tensor_copy(out=rtmp, in_=rtmp2),
                      reads=("sc2",), writes=("sc1",))
            fl.do("vector", lambda v, k=k: v.tensor_copy(
                out=rows2[0:1, k * D:(k + 1) * D], in_=rtmp), reads=("sc1",), writes=("rows",))

        def newton_rsqrt(cbuf, cregion, ibuf, vi, yi, ti_):
            vsl, ysl, tsl = cbuf[:, vi:vi + 1], cbuf[:, yi:yi + 1], cbuf[:, ti_:ti_ + 1]
            isl = ibuf[:]
            fl.do("vector", lambda v: v.tensor_scalar(
                out=isl, in0=vsl.bitcast(I32), scalar1=1, scalar2=None,
                op0=ALU.arith_shift_right), reads=(cregion,), writes=(cregion,))
            fl.do("vector", lambda v: v.tensor_scalar(
                out=isl, in0=isl, scalar1=-1, scalar2=0x5F3759DF, op0=ALU.mult, op1=ALU.add),
                reads=(cregion,), writes=(cregion,))
            fl.do("vector", lambda v: v.tensor_copy(out=ysl, in_=isl.bitcast(F32)),
                  reads=(cregion,), writes=(cregion,))
            for _ in range(3):
                fl.do("vector", lambda v: v.tensor_mul(out=tsl, in0=vsl, in1=ysl),
                      reads=(cregion,), writes=(cregion,))
                fl.do("vector", lambda v: v.tensor_mul(out=tsl, in0=tsl, in1=ysl),
                      reads=(cregion,), writes=(cregion,))
                fl.do("vector", lambda v: v.tensor_scalar(
                    out=tsl, in0=tsl, scalar1=-0.5, scalar2=1.5, op0=ALU.mult, op1=ALU.add),
                    reads=(cregion,), writes=(cregion,))
                fl.do("vector", lambda v: v.tensor_mul(out=ysl, in0=ysl, in1=tsl),
                      reads=(cregion,), writes=(cregion,))

        def bias_rows_mm(psum, preg, rowk, stop, start=False):
            for h in range(2):
                fl.do("tensor", lambda t, h=h, rowk=rowk, stop=stop, psum=psum,
                      start=start: t.matmul(
                    psum[:, h * 512:(h + 1) * 512], ones1f[:],
                    rows2[0:1, rowk * D + h * 512: rowk * D + (h + 1) * 512],
                    start=start, stop=stop), reads=("rows", "consts"), writes=(preg,))

        def phase_elem(pbig, preg):
            fl.do("scalar", lambda s: s.activation(out=sc[0][:], in_=pbig[:], func=AF.Tanh),
                  reads=(preg,), writes=("sc0",))
            fl.do("vector", lambda v: v.tensor_mul(out=sc[1][:], in0=sc[0][:], in1=psB[:]),
                  reads=("sc0", "psB"), writes=("sc1",))
            fl.do("vector", lambda v: v.tensor_scalar(
                out=sc[2][:], in0=sc[1][:], scalar1=MAGIC, scalar2=MAGIC,
                op0=ALU.add, op1=ALU.subtract), reads=("sc1",), writes=("sc2",))
            fl.do("vector", lambda v: v.tensor_sub(out=sc[3][:], in0=sc[1][:], in1=sc[2][:]),
                  reads=("sc1", "sc2"), writes=("sc3",))
            fl.do("vector", lambda v: v.scalar_tensor_tensor(
                out=sc[2][:], in0=sc[3][:], scalar=-1.0, in1=sc[3][:],
                op0=ALU.mult, op1=ALU.max), reads=("sc3",), writes=("sc2",))
            fl.do("scalar", lambda s: s.activation(
                out=sinb[:], in_=sc[3][:], func=AF.Sin, scale=TWO_PI),
                reads=("sc3",), writes=("sinb",))
            fl.do("scalar", lambda s: s.activation(
                out=cosb[:], in_=sc[2][:], func=AF.Sin, bias=halfpi[:], scale=-TWO_PI),
                reads=("sc2", "consts"), writes=("cosb",))

        for ti in range(ntile):
            t0 = ti * TILE
            tp = ti % 2

            for ci in range(NCH):
                r0 = t0 + ci * 128
                ap = ci % 2
                barrier = ()
                if ti == 0 and ci == 0 and os.environ.get("KBAR") == "1":
                    barrier = ("rows", "H", "bkqc", "lngb", "psB", "Wck", "Wcq",
                               "Wv", "Wo", "consts", "bkqbf", "ceT", "exp0", "exp1", "exp2")
                fl.dma("sync", lambda g, r0=r0, ap=ap: g.dma_start(
                    out=xloadA[ap][:], in_=x_h[r0:r0 + 128, :]), reads=barrier,
                    writes=("xlA0",))
                fl.do("vector", lambda v, ap=ap, tp=tp: v.tensor_reduce(
                    out=colsA[tp][:, 0:1], in_=xloadA[ap][:], axis=AX.X, op=ALU.add),
                    reads=("xlA0",), writes=(f"cA{tp}",))
                fl.do("scalar", lambda s, ap=ap: s.activation(
                    out=scA[:], in_=xloadA[ap][:], func=AF.Square),
                    reads=("xlA0",), writes=("scA",))
                fl.do("vector", lambda v, tp=tp: v.tensor_reduce(
                    out=colsA[tp][:, 1:2], in_=scA[:], axis=AX.X, op=ALU.add),
                    reads=("scA",), writes=(f"cA{tp}",))
                fl.do("vector", lambda v, tp=tp: v.tensor_scalar(
                    out=colsA[tp][:, 2:3], in0=colsA[tp][:, 0:1], scalar1=1.0 / D,
                    scalar2=None, op0=ALU.mult), reads=(f"cA{tp}",), writes=(f"cA{tp}",))
                fl.do("vector", lambda v, tp=tp: v.tensor_scalar(
                    out=colsA[tp][:, 3:4], in0=colsA[tp][:, 1:2], scalar1=1.0 / D,
                    scalar2=None, op0=ALU.mult), reads=(f"cA{tp}",), writes=(f"cA{tp}",))
                fl.do("vector", lambda v, tp=tp: v.scalar_tensor_tensor(
                    out=colsA[tp][:, 4:5], in0=colsA[tp][:, 2:3], scalar=colsA[tp][:, 2:3],
                    in1=colsA[tp][:, 3:4], op0=ALU.mult, op1=ALU.subtract),
                    reads=(f"cA{tp}",), writes=(f"cA{tp}",))
                fl.do("vector", lambda v, tp=tp: v.tensor_scalar(
                    out=colsA[tp][:, 4:5], in0=colsA[tp][:, 4:5], scalar1=-1.0, scalar2=1e-5,
                    op0=ALU.mult, op1=ALU.add), reads=(f"cA{tp}",), writes=(f"cA{tp}",))
                newton_rsqrt(colsA[tp], f"cA{tp}", coliA[tp], 4, 5, 6)
                fl.do("vector", lambda v, tp=tp, ci=ci: v.tensor_copy(
                    out=mu_t[tp][:, ci:ci + 1], in_=colsA[tp][:, 2:3]),
                    reads=(f"cA{tp}",), writes=(f"mu{tp}",))
                fl.do("vector", lambda v, tp=tp, ci=ci: v.tensor_copy(
                    out=rs_t[tp][:, ci:ci + 1], in_=colsA[tp][:, 5:6]),
                    reads=(f"cA{tp}",), writes=(f"mu{tp}",))
                fl.do("vector", lambda v, ap=ap, tp=tp: v.tensor_scalar(
                    out=xn_c[ap][:], in0=xloadA[ap][:], scalar1=colsA[tp][:, 2:3],
                    scalar2=colsA[tp][:, 5:6], op0=ALU.subtract, op1=ALU.mult),
                    reads=("xlA0", f"cA{tp}"), writes=(f"xnc{ap}",))
                for h in range(2):
                    fl.do("tensor", lambda t, h=h, ci=ci, ap=ap: t.matmul(
                        bigC[0:NCH, h * 512:(h + 1) * 512], onehot[:, ci * 4:(ci + 1) * 4],
                        xn_c[ap][:, h * 512:(h + 1) * 512],
                        start=(ci == 0), stop=(ci == NCH - 1)),
                        reads=(f"xnc{ap}", "consts"), writes=("bigC",))
            fl.do("vector", lambda v: v.tensor_copy(out=ce8[0:NCH, :], in_=bigC[0:NCH, :]),
                  reads=("bigC",), writes=("ce8",))

            if kcut < 2:
                continue
            for dc in range(NJ):
                fl.do("tensor", lambda t, dc=dc: t.transpose(
                    tr_f[:, 0:8], ce8[:, dc * 128:(dc + 1) * 128], ident8[:]),
                    reads=("ce8", "consts"), writes=("trf",))
                fl.do("vector", lambda v, dc=dc: v.tensor_copy(
                    out=ceT[:, dc * 8:(dc + 1) * 8], in_=tr_f[:, 0:8]),
                    reads=("trf",), writes=("ceT",))
            for dc in range(NJ):
                fl.do("tensor", lambda t, dc=dc: t.matmul(
                    tr_f[0:8, 24:30], ceT[:, dc * 8:(dc + 1) * 8],
                    H_sb[:].rearrange("p (dc j) -> p dc j", j=6)[:, dc, :],
                    start=(dc == 0), stop=(dc == NJ - 1)),
                    reads=("ceT", "H"), writes=("trf",))
            g8 = gate

            def gd(fn):
                fl.do("vector", fn, reads=("g8", "consts"), writes=("g8",))

            fl.do("vector", lambda v: v.tensor_copy(out=g8[:, 0:6], in_=tr_f[0:8, 24:30]),
                  reads=("trf",), writes=("g8",))
            gd(lambda v: v.tensor_scalar(
                out=g8[:, 8:14], in0=g8[:, 0:6], scalar1=0.0, scalar2=None, op0=ALU.is_gt))
            gd(lambda v: v.tensor_mul(out=g8[:, 16:22], in0=g8[:, 8:14], in1=pw8[:]))
            gd(lambda v: v.tensor_reduce(
                out=g8[:, 24:25], in_=g8[:, 16:22], axis=AX.X, op=ALU.add))
            gd(lambda v: v.scalar_tensor_tensor(
                out=g8[:, 16:22], in0=g8[:, 0:6], scalar=-1.0, in1=g8[:, 0:6],
                op0=ALU.mult, op1=ALU.max))
            gd(lambda v: v.tensor_reduce(
                out=g8[:, 25:26], in_=g8[:, 16:22], axis=AX.X, op=ALU.min))
            gd(lambda v: v.tensor_scalar(
                out=g8[:, 32:38], in0=g8[:, 16:22], scalar1=g8[:, 25:26], scalar2=None,
                op0=ALU.is_le))
            gd(lambda v: v.tensor_mul(out=g8[:, 40:46], in0=g8[:, 32:38], in1=iota6b[:]))
            gd(lambda v: v.tensor_scalar(
                out=g8[:, 40:46], in0=g8[:, 40:46], scalar1=1.0e6, scalar2=None, op0=ALU.add))
            gd(lambda v: v.tensor_reduce(
                out=g8[:, 26:27], in_=g8[:, 40:46], axis=AX.X, op=ALU.min))
            gd(lambda v: v.tensor_scalar(
                out=g8[:, 32:38], in0=iota6[:], scalar1=g8[:, 26:27], scalar2=None,
                op0=ALU.is_equal))
            gd(lambda v: v.tensor_mul(out=g8[:, 40:46], in0=g8[:, 32:38], in1=pw8[:]))
            gd(lambda v: v.tensor_reduce(
                out=g8[:, 27:28], in_=g8[:, 40:46], axis=AX.X, op=ALU.add))
            gd(lambda v: v.tensor_mul(out=g8[:, 40:46], in0=g8[:, 32:38], in1=g8[:, 8:14]))
            gd(lambda v: v.tensor_reduce(
                out=g8[:, 28:29], in_=g8[:, 40:46], axis=AX.X, op=ALU.add))
            gd(lambda v: v.tensor_scalar(
                out=g8[:, 29:30], in0=g8[:, 28:29], scalar1=-2.0, scalar2=1.0,
                op0=ALU.mult, op1=ALU.add))
            gd(lambda v: v.tensor_mul(out=g8[:, 29:30], in0=g8[:, 29:30], in1=g8[:, 27:28]))
            gd(lambda v: v.tensor_add(out=g8[:, 30:31], in0=g8[:, 24:25], in1=g8[:, 29:30]))
            fl.do("tensor", lambda t: t.transpose(tr_f[0:1, 32:40], g8[:, 24:25], ident8[:]),
                  reads=("g8", "consts"), writes=("trf",))
            fl.do("vector", lambda v: v.tensor_copy(out=e1row[:], in_=tr_f[0:1, 32:40]),
                  reads=("trf",), writes=("erow",))
            fl.do("tensor", lambda t: t.transpose(tr_f[0:1, 32:40], g8[:, 30:31], ident8[:]),
                  reads=("g8", "erow", "consts"), writes=("trf",))
            fl.do("vector", lambda v: v.tensor_copy(out=e2row[:], in_=tr_f[0:1, 32:40]),
                  reads=("trf",), writes=("erow",))
            fl.do("tensor", lambda t: t.matmul(
                tr_f[:, 8:16], ones1f[:], e1row[:], start=True, stop=True),
                reads=("erow", "consts"), writes=("trf",))
            fl.do("tensor", lambda t: t.matmul(
                tr_f[:, 16:24], ones1f[:], e2row[:], start=True, stop=True),
                reads=("erow", "consts"), writes=("trf",))
            fl.do("vector", lambda v: v.tensor_copy(out=ebc[:, 0:8], in_=tr_f[:, 8:16]),
                  reads=("trf",), writes=("ebc",))
            fl.do("vector", lambda v: v.tensor_copy(out=ebc[:, 8:16], in_=tr_f[:, 16:24]),
                  reads=("trf",), writes=("ebc",))
            for ci in range(NCH):
                for slot in range(2):
                    fl.do("vector", lambda v, ci=ci, slot=slot: v.scalar_tensor_tensor(
                        out=idxp[:, ci * 2 + slot:ci * 2 + slot + 1],
                        in0=ebc[:, slot * 8 + ci:slot * 8 + ci + 1], scalar=128.0,
                        in1=iota_col[:], op0=ALU.mult, op1=ALU.add),
                        reads=("ebc", "consts"), writes=("idx",))
            if os.environ.get("KDBG") == "1":
                fl.do("vector", lambda v: v.tensor_copy(
                    out=sc[1][0:8, 0:64], in_=gate[:, 0:64]), reads=("g8",), writes=("sc1",))
                fl.do("vector", lambda v: v.tensor_copy(
                    out=sc[1][:, 64:68], in_=mu_t[tp][:]), reads=(f"mu{tp}",), writes=("sc1",))
                fl.do("vector", lambda v: v.tensor_copy(
                    out=sc[1][:, 68:72], in_=rs_t[tp][:]), reads=(f"mu{tp}",), writes=("sc1",))
                fl.do("vector", lambda v: v.tensor_copy(
                    out=sc[1][0:8, 72:80], in_=ce8[:, 0:8]), reads=("ce8",), writes=("sc1",))
                fl.do("vector", lambda v: v.tensor_copy(
                    out=sc[1][:, 32:48], in_=xloadA[1][:, 0:16]), reads=("xlA0",), writes=("sc1",))
                fl.dma("sync", lambda g, ti=ti: g.dma_start(
                    out=dbg_h[ti, :, :], in_=sc[1][:, 0:80]), reads=("sc1",),
                    channel="DBG")

            if kcut < 3:
                continue
            for ci in range(NCH):
                r0 = t0 + ci * 128
                c = ti * NCH + ci
                bp = c % 2
                ek0 = (2 * c) % NEXP
                ek1 = (2 * c + 1) % NEXP

                for slot, ek in (() if os.environ.get("KNOGATHER") else ((0, ek0), (1, ek1))):
                    if os.environ.get("KFAKEGATHER") == "1":
                        fl.dma("gpsimd", lambda g, ek=ek: g.dma_start(
                            out=exp_sb[ek][:],
                            in_=ew_rows[0:128, :]),
                            reads=(), writes=(f"exp{ek}",),
                            channel=("GA" if os.environ.get("KCHAINGATHER") else None))
                    else:
                        fl.dma("gpsimd", lambda g, ci=ci, slot=slot, ek=ek: g.indirect_dma_start(
                            out=exp_sb[ek][:], out_offset=None, in_=ew_rows,
                            in_offset=IndirectOffsetOnAxis(
                                idxp[:, ci * 2 + slot:ci * 2 + slot + 1], 0),
                            element_offset=0, bounds_check=bc_reg, oob_is_err=False),
                            reads=("idx",), writes=(f"exp{ek}",),
                            channel=("GA" if os.environ.get("KCHAINGATHER") else None))

                fl.dma("sync", lambda g, r0=r0, bp=bp: g.dma_start(
                    out=xloadB[bp][:], in_=x_h[r0:r0 + 128, :]), writes=(f"xlB{bp}",))
                fl.do("vector", lambda v, bp=bp, tp=tp, ci=ci: v.tensor_scalar(
                    out=xnbf[bp][:], in0=xloadB[bp][:], scalar1=mu_t[tp][:, ci:ci + 1],
                    scalar2=rs_t[tp][:, ci:ci + 1], op0=ALU.subtract, op1=ALU.mult),
                    reads=(f"xlB{bp}", f"mu{tp}"), writes=(f"xnbf{bp}",))
                for j in range(NJ):
                    fl.do("tensor", lambda t, j=j, bp=bp: t.transpose(
                        tr_bf[:], xnbf[bp][:, j::NJ], ident_bf[:]),
                        reads=(f"xnbf{bp}", "consts"), writes=("trbf",))
                    fl.do("vector", lambda v, j=j, bp=bp: v.tensor_copy(
                        out=xnT[bp][:, j * 128:(j + 1) * 128], in_=tr_bf[:]),
                        reads=("trbf",), writes=(f"xnT{bp}",))

                if kcut < 4:
                    continue
                nobias = int(os.environ.get("KNOBIAS", "0"))
                nproj = int(os.environ.get("KNPROJ", "3"))
                for (preg, psum, wnm, rowk) in (
                        ("bigA", bigA, "wv", ROW_BV),
                        ("bigB", bigB, "wck", ROW_BK),
                        ("bigC", bigC, "wcq", ROW_BQ))[:nproj]:
                    wreg = {"wv": "Wv", "wck": "Wck", "wcq": "Wcq"}[wnm]
                    if not nobias:
                        bias_rows_mm(psum, preg, rowk, stop=False, start=True)
                    for j in range(NJ):
                        for h in range(2):
                            fl.do("tensor",
                                  lambda t, j=j, h=h, bp=bp, psum=psum, wnm=wnm,
                                  nobias=nobias: t.matmul(
                                      psum[:, h * 512:(h + 1) * 512],
                                      xnT[bp][:, j * 128:(j + 1) * 128],
                                      W[wnm][:, j * D + h * 512: j * D + (h + 1) * 512],
                                      start=bool(nobias and j == 0),
                                      stop=(j == NJ - 1)),
                                  reads=(f"xnT{bp}", wreg), writes=(preg,))

                if kcut < 5:
                    continue
                ksub = int(os.environ.get("KSUB", "15"))
                if ksub & 1:
                    if os.environ.get("KVALV", "0") == "1":
                        fl.do("vector", lambda v: v.tensor_copy(out=valb[:], in_=bigA[:]),
                              reads=("bigA",), writes=("valb",))
                    else:
                        fl.do("scalar", lambda s: s.activation(
                            out=valb[:], in_=bigA[:], func=AF.Copy, scale=1.0),
                            reads=("bigA",), writes=("valb",))
                if ksub & 2:
                    phase_elem(bigB, "bigB")
                if ksub & 4:
                    fl.do("vector", lambda v: v.tensor_mul(out=br_sb[:], in0=valb[:], in1=cosb[:]),
                          reads=("valb", "cosb"), writes=("br",))
                    fl.do("vector", lambda v: v.tensor_mul(out=bi_sb[:], in0=valb[:], in1=sinb[:]),
                          reads=("valb", "sinb"), writes=("bi",))
                if ksub & 8:
                    for h in range(2):
                        fl.do("tensor", lambda t, h=h: t.matmul(
                            bigA[:, h * 512:(h + 1) * 512], ublk[:], br_sb[:, h * 512:(h + 1) * 512],
                            start=True, stop=True), reads=("br", "consts"), writes=("bigA",))
                if kcut < 6:
                    continue
                phase_elem(bigC, "bigC")
                for h in range(2):
                    fl.do("tensor", lambda t, h=h: t.matmul(
                        bigB[:, h * 512:(h + 1) * 512], ublk[:], bi_sb[:, h * 512:(h + 1) * 512],
                        start=True, stop=True), reads=("bi", "consts"), writes=("bigB",))
                if kcut < 7:
                    continue
                fl.do("vector", lambda v: v.tensor_mul(out=sc[1][:], in0=bigA[:], in1=cosb[:]),
                      reads=("bigA", "cosb"), writes=("sc1",))
                fl.do("vector", lambda v: v.tensor_mul(out=sc[2][:], in0=bigB[:], in1=sinb[:]),
                      reads=("bigB", "sinb"), writes=("sc2",))
                fl.do("vector", lambda v: v.scalar_tensor_tensor(
                    out=sc[3][:], in0=sc[1][:], scalar=1.0, in1=sc[2][:],
                    op0=ALU.mult, op1=ALU.add, accum_out=colsB[:, 0:1]),
                    reads=("sc1", "sc2"), writes=("sc3", "cB"))
                fl.do("scalar", lambda s: s.activation(
                    out=sc[1][:], in_=sc[3][:], func=AF.Square),
                    reads=("sc3",), writes=("sc1",))
                fl.do("vector", lambda v: v.tensor_reduce(
                    out=colsB[:, 1:2], in_=sc[1][:], axis=AX.X, op=ALU.add),
                    reads=("sc1",), writes=("cB",))
                fl.do("vector", lambda v: v.tensor_scalar(
                    out=colsB[:, 2:3], in0=colsB[:, 0:1], scalar1=1.0 / D, scalar2=None,
                    op0=ALU.mult), reads=("cB",), writes=("cB",))
                fl.do("vector", lambda v: v.tensor_scalar(
                    out=colsB[:, 3:4], in0=colsB[:, 1:2], scalar1=1.0 / D, scalar2=None,
                    op0=ALU.mult), reads=("cB",), writes=("cB",))
                fl.do("vector", lambda v: v.scalar_tensor_tensor(
                    out=colsB[:, 4:5], in0=colsB[:, 2:3], scalar=colsB[:, 2:3],
                    in1=colsB[:, 3:4], op0=ALU.mult, op1=ALU.subtract),
                    reads=("cB",), writes=("cB",))
                fl.do("vector", lambda v: v.tensor_scalar(
                    out=colsB[:, 4:5], in0=colsB[:, 4:5], scalar1=-1.0 / 1024.0, scalar2=1e-5,
                    op0=ALU.mult, op1=ALU.add), reads=("cB",), writes=("cB",))
                newton_rsqrt(colsB, "cB", coliB, 4, 5, 6)
                fl.do("vector", lambda v: v.tensor_scalar(
                    out=colsB[:, 5:6], in0=colsB[:, 5:6], scalar1=1.0 / 32.0, scalar2=None,
                    op0=ALU.mult), reads=("cB",), writes=("cB",))
                fl.do("vector", lambda v, bp=bp: v.tensor_scalar(
                    out=xnbf[bp][:], in0=sc[3][:], scalar1=colsB[:, 2:3],
                    scalar2=colsB[:, 5:6], op0=ALU.subtract, op1=ALU.mult),
                    reads=("sc3", "cB"), writes=(f"xnbf{bp}",))
                for j in range(NJ):
                    fl.do("tensor", lambda t, j=j, bp=bp: t.transpose(
                        tr_bf[:], xnbf[bp][:, j::NJ], ident_bf[:]),
                        reads=(f"xnbf{bp}", "consts"), writes=("trbf",))
                    if os.environ.get("KZTC", "0") == "1":
                        fl.do("vector", lambda v, j=j: v.tensor_copy(
                            out=zT[:, j * 128:(j + 1) * 128], in_=tr_bf[:]),
                            reads=("trbf",), writes=("zT",))
                    else:
                        fl.do("vector", lambda v, j=j: v.tensor_scalar(
                            out=zT[:, j * 128:(j + 1) * 128], in0=tr_bf[:],
                            scalar1=lngb_col[:, j:j + 1], scalar2=None, op0=ALU.mult),
                            reads=("trbf", "lngb"), writes=("zT",))
                for j in range(NJ):
                    for h in range(2):
                        fl.do("tensor", lambda t, j=j, h=h: t.matmul(
                            bigA[:, h * 512:(h + 1) * 512], zT[:, j * 128:(j + 1) * 128],
                            W["wo"][:, j * D + h * 512: j * D + (h + 1) * 512],
                            start=(j == 0), stop=False),
                            reads=("zT", "Wo"), writes=("bigA",))
                bias_rows_mm(bigA, "bigA", ROW_BO, stop=True)
                fl.do("vector", lambda v, bp=bp, tp=tp, ci=ci: v.tensor_scalar(
                    out=xn_b[:], in0=xloadB[bp][:], scalar1=mu_t[tp][:, ci:ci + 1],
                    scalar2=rs_t[tp][:, ci:ci + 1], op0=ALU.subtract, op1=ALU.mult),
                    reads=(f"xlB{bp}", f"mu{tp}"), writes=("scA",))
                fl.do("vector", lambda v: v.tensor_add(out=sc[3][:], in0=xn_b[:], in1=bigA[:]),
                      reads=("scA", "bigA"), writes=("sc3",))
                if kcut < 8:
                    continue
                for slot, ek in ((0, ek0), (1, ek1)):
                    for j in range(NJ):
                        for h in range(2):
                            fl.do("tensor",
                                  lambda t, j=j, h=h, bp=bp, ek=ek, slot=slot: t.matmul(
                                      bigC[:, h * 512:(h + 1) * 512],
                                      xnT[bp][:, j * 128:(j + 1) * 128],
                                      exp_sb[ek][:, j * D + h * 512: j * D + (h + 1) * 512],
                                      start=(slot == 0 and j == 0),
                                      stop=(slot == 1 and j == NJ - 1)),
                                  reads=(f"xnT{bp}", f"exp{ek}"), writes=("bigC",))
                fl.do("scalar", lambda s: s.activation(
                    out=sc[0][:], in_=bigC[:], func=AF.Silu, scale=0.5),
                    reads=("bigC",), writes=("sc0",))
                fl.do("vector", lambda v: v.tensor_mul(out=sc[1][:], in0=sc[3][:], in1=sc[0][:]),
                      reads=("sc3", "sc0"), writes=("sc1",))
                fl.do("vector", lambda v, bp=bp: v.tensor_add(
                    out=sc[2][:], in0=xloadB[bp][:], in1=sc[1][:]),
                    reads=(f"xlB{bp}", "sc1"), writes=("sc2",))
                fl.dma("sync", lambda g, r0=r0: g.dma_start(
                    out=out_h[r0:r0 + 128, :], in_=sc[2][:]), reads=("sc2",),
                    channel="OUT")

        fl.emit(blk)
    return nc


_NC_CACHE = {}


def kernel(**inputs):
    x = np.ascontiguousarray(np.asarray(inputs["L0_attn_x"], np.float32))
    B = x.shape[0]
    shared = {}
    for k, src in (("wvT", "L0_attn_Wv"), ("wkpT", "L0_attn_Wkp"),
                   ("wqpT", "L0_attn_Wqp"), ("woT", "L0_attn_Wo")):
        shared[k] = np.ascontiguousarray(np.asarray(inputs[src], np.float32).T)
    for k, src in (("wkG", "L0_attn_Wk"), ("wqG", "L0_attn_Wq")):
        shared[k] = np.ascontiguousarray(np.asarray(inputs[src], np.float32))
    for k, src in (("bk", "L0_attn_bk"), ("bv", "L0_attn_bv"), ("bq", "L0_attn_bq"),
                   ("bkp", "L0_attn_bkp"), ("bqp", "L0_attn_bqp"),
                   ("ps", "L0_attn_phase_scale"), ("lng", "L0_attn_ln_g"),
                   ("lnb", "L0_attn_ln_b"), ("bo", "L0_attn_bo")):
        shared[k] = np.ascontiguousarray(np.asarray(inputs[src], np.float32))
    shared["hyp"] = np.ascontiguousarray(np.asarray(inputs["L0_ffn_hyperplanes"], np.float32))
    shared["ew"] = np.ascontiguousarray(np.asarray(inputs["L0_ffn_expert_weights"], np.float32))

    if "nc" not in _NC_CACHE:
        _NC_CACHE["nc"] = build_program()
    nc = _NC_CACHE["nc"]

    in_maps = []
    for b in range(B):
        m = dict(shared)
        m["x"] = np.ascontiguousarray(x[b])
        in_maps.append(m)
    res = run_bass_kernel_spmd(nc, in_maps, core_ids=list(range(B)))
    out = np.stack([np.asarray(r["out"], np.float32) for r in res.results], axis=0)
    return out
